# version 03
# baseline (speedup 1.0000x reference)
import math
from contextlib import ExitStack

import numpy as np
import ml_dtypes

import concourse.bass as bass
import concourse.mybir as mybir
from concourse.bass_utils import run_bass_kernel_spmd

AF = mybir.ActivationFunctionType
ALU = mybir.AluOpType
F32 = mybir.dt.float32
BF16 = mybir.dt.bfloat16
I32 = mybir.dt.int32
BF = ml_dtypes.bfloat16

B, S, D = 4, 2048, 4096
TOKH = S // 2
NCORE = 8
PERM = np.concatenate([np.arange(0, 128, 2), np.arange(1, 128, 2)])
SCALE_ATT = 1.0 / math.sqrt(128.0)
NEGBIG = -1.0e9

FUSED = False

L0A_NKVH = 4
L0A_NQT = 16
L0A_FEAT = set()
L1P_NTG = 8
L1P_FEAT = set()


class P:

    ENGS = ["sync", "gpsimd", "tensor", "vector", "scalar"]

    def __init__(self, name):
        self.nc = bass.Bass("TRN2", target_bir_lowering=False)
        self.steps = []
        self.stack = ExitStack()
        self._n = 0
        self._pool = None
        self._ccn = 0

    @property
    def pool(self):
        if self._pool is None:
            self._pool = {
                "A": self.ps("poolA", [128, 512], F32),
                "B": self.ps("poolB", [128, 512], F32),
                "C": self.ps("poolC", [128, 512], F32),
                "D": self.ps("poolD", [128, 512], F32),
                "T16": self.ps("poolT16", [128, 128], BF16),
                "TF": self.ps("poolTF", [128, 128], F32),
                "AO": self.ps("poolAO", [128, 128], F32),
                "P1": self.ps("poolP1", [1, 128], F32),
            }
        return self._pool

    def sb(self, name, shape, dt, stack=None):
        return (stack or self.stack).enter_context(self.nc.sbuf_tensor(name, shape, dt))

    def ps(self, name, shape, dt=F32):
        return self.stack.enter_context(self.nc.psum_tensor(name, shape, dt))

    def din(self, name, shape, dt):
        return self.nc.dram_tensor(name, shape, dt, kind="ExternalInput").ap()

    def dout(self, name, shape, dt):
        return self.nc.dram_tensor(name, shape, dt, kind="ExternalOutput").ap()

    def dint(self, name, shape, dt):
        return self.nc.dram_tensor(name, shape, dt, kind="Internal").ap()

    def add(self, eng, fn, inc=1):
        self.steps.append((eng, inc, fn))

    def dma(self, out, in_):
        eng = "gpsimd" if out.dtype != in_.dtype else "sync"
        self.add(eng, lambda e, o=out, i=in_: e.dma_start(out=o, in_=i), inc=16)

    def dma_g(self, out, in_):
        self.add("gpsimd", lambda e, o=out, i=in_: e.dma_start(out=o, in_=i), inc=16)

    def mm(self, out, lhsT, rhs, start=True, stop=True):
        self.add("tensor", lambda e, o=out, l=lhsT, r=rhs, s1=start, s2=stop:
                 e.matmul(o, l, r, start=s1, stop=s2))

    def trans(self, out, in_, ident):
        self.add("tensor", lambda e, o=out, i=in_, d=ident: e.transpose(o, i, d))

    def tt(self, out, a, b, op):
        self.add("vector", lambda e, o=out, x=a, y=b, p=op: e.tensor_tensor(o, x, y, p))

    def ts(self, out, a, s1, op0, s2=None, op1=None):
        if op1 is None:
            self.add("vector", lambda e, o=out, x=a, s=s1, p=op0:
                     e.tensor_scalar(o, x, s, None, p))
        else:
            self.add("vector", lambda e, o=out, x=a, sa=s1, sb=s2, p=op0, q=op1:
                     e.tensor_scalar(o, x, sa, sb, p, q))

    def stt(self, out, in0, scalar, in1, op0, op1):
        self.add("vector", lambda e, o=out, a=in0, s=scalar, b=in1, p=op0, q=op1:
                 e.scalar_tensor_tensor(o, a, s, b, p, q))

    def act(self, out, in_, func, bias=0.0, scale=1.0, accum_out=None):
        if accum_out is None:
            self.add("scalar", lambda e, o=out, i=in_, f=func, b=bias, s=scale:
                     e.activation(o, i, f, bias=b, scale=s))
        else:
            self.add("scalar", lambda e, o=out, i=in_, f=func, b=bias, s=scale, a=accum_out:
                     e.activation(o, i, f, bias=b, scale=s, accum_out=a))

    def copy(self, out, in_):
        self.add("vector", lambda e, o=out, i=in_: e.tensor_copy(o, i))

    def scopy(self, out, in_):
        self.add("scalar", lambda e, o=out, i=in_: e.copy(o, i))

    def rmax(self, out, in_):
        self.add("vector", lambda e, o=out, i=in_:
                 e.reduce_max(o, i, axis=mybir.AxisListType.X))

    def rsum(self, out, in_):
        self.add("vector", lambda e, o=out, i=in_:
                 e.reduce_sum(o, i, axis=mybir.AxisListType.X))

    def recip(self, out, in_):
        self.add("vector", lambda e, o=out, i=in_: e.reciprocal(o, i))

    def max8(self, out, in_):
        self.add("vector", lambda e, o=out, i=in_: e.max(out=o, in_=i))

    def memset(self, ap, v):
        self.add("vector", lambda e, a=ap, c=v: e.memset(a, c))

    def collective(self, kind, op, groups, ins, outs):
        self._ccn += 1
        ccs = self.stack.enter_context(self.nc.semaphore(f"ccs{self._ccn}"))

        def fn(e, k=kind, o=op, g=groups, i=ins, u=outs, s=ccs):
            e.collective_compute(k, o, replica_groups=g, ins=i, outs=u).then_inc(s)
            e.wait_ge(s, 1)
            return e.nop()
        self.add("gpsimd", fn, inc=1)

    def finalize(self):
        nc = self.nc
        LIMIT = 400_000
        segs = []
        cur = []
        tot = 0
        for st in self.steps:
            if tot + st[1] > LIMIT:
                segs.append(cur)
                cur = []
                tot = 0
            cur.append(st)
            tot += st[1]
        segs.append(cur)
        sems = [self.stack.enter_context(nc.semaphore(f"seq{i}")) for i in range(len(segs))]
        seg_cums = []
        for sg in segs:
            c = [0]
            for (_, inc, _) in sg:
                c.append(c[-1] + inc)
            seg_cums.append(c)
        blk = self.stack.enter_context(nc.Block())

        def make(engname):
            def body(eng):
                for si, sg in enumerate(segs):
                    first_in_seg = True
                    for i, (en, inc, fn) in enumerate(sg):
                        if en != engname:
                            continue
                        if first_in_seg and si > 0:
                            eng.wait_ge(sems[si - 1], seg_cums[si - 1][-1])
                            first_in_seg = False
                        if seg_cums[si][i] > 0:
                            eng.wait_ge(sems[si], seg_cums[si][i])
                        elif si > 0:
                            eng.wait_ge(sems[si - 1], seg_cums[si - 1][-1])
                        fn(eng).then_inc(sems[si], inc)
                eng.wait_ge(sems[-1], seg_cums[-1][-1])
            return body

        for n in self.ENGS:
            getattr(blk, n)(make(n))
        self.stack.close()
        return nc


def _rope_tables():
    inv = 1.0 / 10000.0 ** (np.arange(0, 128, 2, dtype=np.float64) / 128.0)
    ang = np.arange(S, dtype=np.float64)[None, :] * inv[:, None]
    return np.cos(ang).astype(np.float32), np.sin(ang).astype(np.float32)


def _consts():
    c = {}
    c["identf"] = np.eye(128, dtype=np.float32)
    ropC, ropS = _rope_tables()
    c["ropC"] = np.concatenate([ropC, ropC], 0)
    c["ropS"] = np.concatenate([-ropS, ropS], 0)
    sw = np.zeros((128, 128), np.float32)
    sw[np.arange(64), np.arange(64) + 64] = 1.0
    sw[np.arange(64) + 64, np.arange(64)] = 1.0
    c["swp"] = sw
    ck_seq = np.concatenate([[-1], (np.arange(128) + 1) * 16 - 1])
    cg = (ck_seq[None, :] < np.arange(S)[:, None]).astype(np.float32)
    c["cgate"] = cg.reshape(16, 128, 129).transpose(1, 0, 2).reshape(128, 16 * 129).astype(BF)
    pp = np.arange(128)
    c["owndiag"] = (((pp[None, :] // 16) == (pp[:, None] // 16)) &
                    (pp[None, :] <= pp[:, None])).astype(BF)
    c["banddiag"] = ((pp[None, :] <= pp[:, None]) & (pp[None, :] >= pp[:, None] - 64)).astype(BF)
    c["bandprev"] = (pp[None, :] >= pp[:, None] + 64).astype(BF)
    c["utones"] = np.triu(np.ones((128, 128), np.float32))
    c["maskaddC"] = np.where(pp[:, None] <= pp[None, :], 0.0, NEGBIG).astype(np.float32)
    return c


def prep_core(inp, core):
    b, s = core // 2, core % 2
    g = {}
    x = np.ascontiguousarray(inp["L0_attn_hidden_states"][b])
    g["xb"] = x
    W = inp["L0_attn_w_qkv"]
    blocks = []
    for i in range(16):
        h = 16 * s + i
        blocks.append(W[h * 128 + PERM, :])
    for j in range(4):
        kvh = 4 * s + j
        blocks.append(W[4096 + kvh * 128 + PERM, :])
    for j in range(4):
        kvh = 4 * s + j
        blocks.append(W[5120 + kvh * 128 :5120 + kvh * 128 + 128, :])
    blocks.append(inp["L0_attn_w_gate"][48 * s : 48 * s + 48, :])
    blocks.append(np.zeros((80, D), np.float32))
    wq = np.concatenate(blocks, 0)
    g["wqkvgT"] = np.ascontiguousarray(wq.reshape(25, 128, D).transpose(0, 2, 1))
    g["b_gate"] = np.ascontiguousarray(inp["L0_attn_b_gate"][48 * s : 48 * s + 48, None])

    of = slice(2048 * s, 2048 * s + 2048)
    g["xbf"] = np.ascontiguousarray(x[:, of])
    g["qweight"] = np.ascontiguousarray(inp["L0_ffn_qweight"].reshape(4096, 4096)[of])
    g["qscales"] = np.ascontiguousarray(inp["L0_ffn_scales"][of, :, 0])
    g["qzeros"] = np.ascontiguousarray(inp["L0_ffn_scaled_zeros"][of, :, 0])
    g["fbias"] = np.ascontiguousarray(inp["L0_ffn_bias"][None, of])

    inperm = (np.arange(16)[:, None] * 128 + PERM[None, :]).reshape(-1)
    kw1 = inp["L0_attn_k_w1"].T[inperm, :]
    g["kw1T"] = np.ascontiguousarray(kw1.reshape(2048, 16, 128).transpose(1, 0, 2))
    vw1 = inp["L0_attn_v_w1"].T
    g["vw1T"] = np.ascontiguousarray(vw1.reshape(2048, 16, 128).transpose(1, 0, 2))
    g["kw2T"] = np.ascontiguousarray(inp["L0_attn_k_w2"].T[:, PERM])
    g["vw2T"] = np.ascontiguousarray(inp["L0_attn_v_w2"].T)
    g["kb1"] = np.ascontiguousarray(inp["L0_attn_k_b1"].reshape(16, 128).T)
    g["vb1"] = np.ascontiguousarray(inp["L0_attn_v_b1"].reshape(16, 128).T)
    g["kb2"] = np.ascontiguousarray(inp["L0_attn_k_b2"][PERM, None])
    g["vb2"] = np.ascontiguousarray(inp["L0_attn_v_b2"][:, None])
    kvh_own = [4 * s + j for j in range(4)]
    g["kposT"] = np.ascontiguousarray(
        np.concatenate([inp["L0_attn_k_pos"][kvh].T[PERM, :] for kvh in kvh_own], 1))
    g["vposT"] = np.ascontiguousarray(
        np.concatenate([inp["L0_attn_v_pos"][kvh].T for kvh in kvh_own], 1))
    g["memk"] = np.ascontiguousarray(
        np.stack([inp["L0_attn_mem_kv"][0, kvh, 0][PERM] for kvh in kvh_own], 1))
    g["memv"] = np.ascontiguousarray(
        np.concatenate([inp["L0_attn_mem_kv"][1, kvh, 0] for kvh in kvh_own])[None, :])

    g["woT"] = np.ascontiguousarray(
        inp["L0_attn_w_o"].T[:, of].reshape(4096, 4, 512).transpose(1, 0, 2))

    Pw = inp["L1_attn_in_proj_w"]
    INTER = 8192
    CONV = 10240
    hs_rows = Pw[INTER + 4096 * s : INTER + 4096 * s + 4096, :]
    B_rows = Pw[INTER + CONV - 2048 : INTER + CONV - 1024, :]
    C_rows = Pw[INTER + CONV - 1024 : INTER + CONV, :]
    gate_rows = Pw[4096 * s : 4096 * s + 4096, :]
    dt_rows = Pw[INTER + CONV + 64 * s : INTER + CONV + 64 * s + 64, :]
    ipw = np.concatenate([hs_rows, B_rows, C_rows, gate_rows, dt_rows, np.zeros((64, D), np.float32)], 0)
    g["inprojT"] = np.ascontiguousarray(ipw.reshape(81, 128, D).transpose(0, 2, 1))
    conv_idx = np.concatenate([np.arange(4096 * s, 4096 * s + 4096),
                               np.arange(8192, 10240)])
    cw = inp["L1_attn_conv_w"][conv_idx, 0, :]
    g["convw"] = np.ascontiguousarray(cw.reshape(48, 128, 4).transpose(1, 0, 2).reshape(128, 192))
    cb = inp["L1_attn_conv_b"][conv_idx]
    g["convb"] = np.ascontiguousarray(cb.reshape(48, 128).T)
    g["dtbias"] = np.ascontiguousarray(inp["L1_attn_dt_bias"][64 * s : 64 * s + 64, None])
    g["alog"] = np.ascontiguousarray(inp["L1_attn_A_log"][64 * s : 64 * s + 64, None])
    g["dvals"] = np.ascontiguousarray(
        np.broadcast_to(inp["L1_attn_D"][None, 64 * s : 64 * s + 64], (128, 64)))
    g["nw"] = np.ascontiguousarray(
        inp["L1_attn_norm_weight"][4096 * s : 4096 * s + 4096].reshape(32, 128).T)
    g["opT"] = np.ascontiguousarray(
        inp["L1_attn_out_proj_w"].T[:, of].reshape(8192, 4, 512).transpose(1, 0, 2))

    g["w1T"] = np.ascontiguousarray(
        inp["L1_ffn_W"].T[:, of].reshape(4096, 4, 512).transpose(1, 0, 2))
    g["bias1"] = np.ascontiguousarray(inp["L1_ffn_bias"][None, of])
    g["loraAT"] = np.ascontiguousarray(inp["L1_ffn_lora_A"].reshape(128, 4096).T)
    g["loraBT"] = np.ascontiguousarray(
        inp["L1_ffn_lora_B"].transpose(0, 2, 1).reshape(128, 4096)[:, of])
    g["coef"] = np.ascontiguousarray(np.repeat(inp["L1_ffn_lora_weights"], 16)[:, None])
    return g



def _ctx(p):
    ctx = {}
    idin = p.din("identf", [128, 128], F32)
    ctx["identf"] = p.sb("c_identf", [128, 128], F32)
    ctx["ident16"] = p.sb("c_ident16", [128, 128], BF16)
    p.dma(ctx["identf"][:], idin)
    p.dma_g(ctx["ident16"][:], idin)
    return ctx


def build_l0pre(p, ctx, io):
    xb, xbf, wq, qw, qs, qz, fb = (io["xb"], io["xbf"], io["wqkvgT"], io["qweight"],
                                   io["qscales"], io["qzeros"], io["fbias"])
    qkvgT, ffnacc = io["qkvgT"], io["ffnacc"]
    identf, ident16 = ctx["identf"], ctx["ident16"]
    stk = ExitStack()
    _sb = p.sb
    p_sb = lambda n, s, d: _sb(n, s, d, stk)

    HT = TOKH
    xnT = p_sb("xnT", [128, 32 * HT], BF16)
    xtile = p_sb("xtile", [128, D], F32)
    stat = p_sb("stat", [128, 8], F32)
    xnb = p_sb("xnb", [128, D], BF16)
    ptr = p.pool["T16"]
    wcol = p_sb("wcol", [128, 2 * 32 * 128], BF16)
    stg = p_sb("stg", [128, HT], BF16)
    pq = [p.pool["A"], p.pool["B"]]

    qwt = p_sb("qwt", [128, 4096], I32)
    qsc = p_sb("qsc", [128, 32], F32)
    qze = p_sb("qze", [128, 32], F32)
    dqb = p_sb("dqb", [128, 4096], BF16)
    wdqT = p_sb("wdqT", [128, 32 * 512], BF16)
    ones1 = p_sb("ones1", [1, 128], BF16)
    fbrow = p_sb("fbrow", [1, 2048], BF16)
    xres = p_sb("xres", [128, 512], F32)
    facc = p_sb("facc", [128, 512], F32)
    pf = p.pool["C"]
    p.memset(ones1[:], 1.0)
    p.dma_g(fbrow[:], fb)

    for hf in range(2):
        t0 = hf * HT
        for tt in range(8):
            p.dma(xtile[:], xb[t0 + tt * 128 : t0 + tt * 128 + 128, :])
            p.act(xnb[:], xtile[:], AF.Square, accum_out=stat[:, 1:2])
            p.rsum(stat[:, 0:1], xtile[:])
            p.ts(stat[:, 0:1], stat[:, 0:1], 1.0 / D, ALU.mult)
            p.ts(stat[:, 2:3], stat[:, 1:2], 1.0 / D, ALU.mult)
            p.ts(stat[:, 4:5], stat[:, 0:1], stat[:, 0:1], ALU.mult)
            p.tt(stat[:, 3:4], stat[:, 2:3], stat[:, 4:5], ALU.subtract)
            p.ts(stat[:, 3:4], stat[:, 3:4], 1e-5, ALU.add)
            p.add("scalar", lambda e, o=stat[:, 5:6], i=stat[:, 3:4]: e.sqrt(o, i))
            p.recip(stat[:, 6:7], stat[:, 5:6])
            p.ts(xnb[:], xtile[:], stat[:, 0:1], ALU.subtract, stat[:, 6:7], ALU.mult)
            for k in range(32):
                p.trans(ptr[:], xnb[:, k * 128 : k * 128 + 128], ident16[:])
                p.copy(xnT[:, k * HT + tt * 128 : k * HT + tt * 128 + 128], ptr[:])
        for ct in range(25):
            wc = wcol[:, (ct % 2) * 4096 : (ct % 2) * 4096 + 4096]
            p.dma_g(wc.rearrange("p (k c) -> p k c", k=32),
                    wq[ct].rearrange("(k p) c -> p k c", k=32))
            for k in range(32):
                for tg in range(2):
                    p.mm(pq[tg][:], wc[:, k * 128 : k * 128 + 128],
                         xnT[:, k * HT + tg * 512 : k * HT + tg * 512 + 512],
                         start=(k == 0), stop=(k == 31))
            for tg in range(2):
                p.copy(stg[:, tg * 512 : tg * 512 + 512], pq[tg][:])
            p.dma(qkvgT[ct * 128 : ct * 128 + 128, t0 : t0 + HT], stg[:])

        for ot in range(4):
            for oc in range(4):
                orow = ot * 512 + oc * 128
                p.dma(qwt[:], qw[orow : orow + 128, :])
                p.dma(qsc[:], qs[orow : orow + 128, :])
                p.dma(qze[:], qz[orow : orow + 128, :])
                for gg in range(32):
                    p.act(dqb[:, gg * 128 : gg * 128 + 128], qwt[:, gg * 128 : gg * 128 + 128],
                          AF.Identity, bias=qze[:, gg : gg + 1], scale=qsc[:, gg : gg + 1])
                for gg in range(32):
                    p.trans(ptr[:], dqb[:, gg * 128 : gg * 128 + 128], ident16[:])
                    p.copy(wdqT[:, gg * 512 + oc * 128 : gg * 512 + oc * 128 + 128], ptr[:])
            for tt in range(8):
                for k in range(32):
                    p.mm(pf[:], xnT[:, k * HT + tt * 128 : k * HT + tt * 128 + 128],
                         wdqT[:, k * 512 : k * 512 + 512], start=(k == 0), stop=False)
                p.mm(pf[:], ones1[:], fbrow[:, ot * 512 : ot * 512 + 512], start=False, stop=True)
                p.dma(xres[:], xbf[t0 + tt * 128 : t0 + tt * 128 + 128, ot * 512 : ot * 512 + 512])
                p.tt(facc[:], pf[:], xres[:], ALU.add)
                p.dma(ffnacc[t0 + tt * 128 : t0 + tt * 128 + 128, ot * 512 : ot * 512 + 512], facc[:])
    stk.close()
    return p


def build_l0attn(p, ctx, io):
    (qkvgT, kw1, vw1, kw2, vw2, kb1i, vb1i, kb2i, vb2i, kposi, vposi, memki,
     memvi, bgati, ropci, ropsi, cgati, owndi, banddi, bandpi, aoT) = (
        io["qkvgT"], io["kw1T"], io["vw1T"], io["kw2T"], io["vw2T"], io["kb1"],
        io["vb1"], io["kb2"], io["vb2"], io["kposT"], io["vposT"], io["memk"],
        io["memv"], io["b_gate"], io["ropC"], io["ropS"], io["cgate"],
        io["owndiag"], io["banddiag"], io["bandprev"], io["aoT"])
    stk = ExitStack()
    _sb = p.sb
    p_sb = lambda n, s, d, st=None: _sb(n, s, d, st or stk)

    identf, ident16 = ctx["identf"], ctx["ident16"]
    ropC = p_sb("ropC_s", [128, S], F32)
    ropS = p_sb("ropS_s", [128, S], F32)
    cgate = p_sb("cgate_s", [128, 16 * 129], BF16)
    ownd = p_sb("ownd_s", [128, 128], BF16)
    bandd = p_sb("bandd_s", [128, 128], BF16)
    bandp = p_sb("bandp_s", [128, 128], BF16)
    kposT = p_sb("kposT_s", [128, 64], F32)
    vposT = p_sb("vposT_s", [128, 64], F32)
    memk = p_sb("memk_s", [128, 4], F32)
    mv_s = p_sb("mv_s", [1, 512], BF16)
    kb1 = p_sb("kb1_s", [128, 16], F32)
    vb1 = p_sb("vb1_s", [128, 16], F32)
    kb2 = p_sb("kb2_s", [128, 1], F32)
    vb2 = p_sb("vb2_s", [128, 1], F32)
    bgat = p_sb("bgat_s", [48, 1], F32)
    for dst, src in [(ropC, ropci), (ropS, ropsi), (cgate, cgati),
                     (ownd, owndi), (bandd, banddi), (bandp, bandpi), (kposT, kposi),
                     (vposT, vposi), (memk, memki), (kb1, kb1i), (vb1, vb1i),
                     (kb2, kb2i), (vb2, vb2i), (bgat, bgati)]:
        p.dma(dst[:], src)
    swpi = io["swp"]
    p.dma_g(mv_s[:], memvi)

    rkT = p_sb("rkT", [128, 4 * S], BF16)
    V_all = p_sb("V_all", [128, 4 * S], BF16)
    ckT = p_sb("ckT", [128, 4 * 129], BF16)
    cvT = p_sb("cvT", [128, 4 * 129], BF16)
    cv128 = p_sb("cv128", [128, 4 * 128], BF16)
    gatesT = p_sb("gatesT", [128, 16 * 48], F32)
    csimS = p_sb("csimS", [128, 4 * 129], F32)
    rsc1 = p_sb("rsc1", [128, 512], F32)
    rsc2 = p_sb("rsc2", [128, 512], F32)
    swp = p_sb("swp_s", [128, 128], BF16)
    p.dma_g(swp[:], swpi)
    aoTkv = p_sb("aoTkv", [128, 4 * S], BF16)

    psS = [p.pool["A"], p.pool["B"], p.pool["C"], p.pool["D"]]
    ps_ao = p.pool["AO"]
    ptr16 = p.pool["T16"]
    ptrf = p.pool["TF"]
    pt1 = p.pool["T16"][0:1, 0:128]

    def rope_to(dst, src, t0, n):
        for c0 in range(0, n, 512):
            w = min(512, n - c0)
            p.mm(psS[3][:, 0:w], swp[:], src[:, c0 : c0 + w])
            p.tt(rsc1[:, 0:w], src[:, c0 : c0 + w], ropC[:, t0 + c0 : t0 + c0 + w], ALU.mult)
            p.tt(rsc2[:, 0:w], psS[3][:, 0:w], ropS[:, t0 + c0 : t0 + c0 + w], ALU.mult)
            p.tt(dst[:, c0 : c0 + w], rsc1[:, 0:w], rsc2[:, 0:w], ALU.add)

    sa_stack = ExitStack()
    ktmp = p_sb("ktmp", [128, S], BF16, sa_stack)
    kpcT = p_sb("kpcT", [128, 4 * S], BF16, sa_stack)
    hidT = p_sb("hidT", [128, 16 * 512], BF16, sa_stack)
    w1s = p_sb("w1s", [128, 16 * 128], BF16, sa_stack)
    w2s = p_sb("w2s", [128, 16 * 128], BF16, sa_stack)

    def compress(w1, w2, b1, b2, dstT):
        for ft in range(16):
            p.dma_g(w1s[:].rearrange("p (ki f) -> p ki f", ki=16),
                    w1[ft].rearrange("(ki p) f -> p ki f", ki=16))
            for Ki in range(16):
                p.mm(psS[0][:], w1s[:, Ki * 128 : Ki * 128 + 128],
                     kpcT[:, Ki * 512 : Ki * 512 + 512], start=(Ki == 0), stop=(Ki == 15))
            p.act(hidT[:, ft * 512 : ft * 512 + 512], psS[0][:], AF.Relu,
                  bias=b1[:, ft : ft + 1], scale=1.0)
        p.dma_g(w2s[:].rearrange("p (kf d) -> p kf d", kf=16),
                w2.rearrange("(kf p) d -> p kf d", kf=16))
        for Kf in range(16):
            p.mm(psS[1][:], w2s[:, Kf * 128 : Kf * 128 + 128],
                 hidT[:, Kf * 512 : Kf * 512 + 512], start=(Kf == 0), stop=(Kf == 15))
        for j in range(4):
            p.act(dstT[:, j * 129 + 1 : j * 129 + 129], psS[1][:, j * 128 : j * 128 + 128],
                  AF.Identity, bias=b2[:], scale=1.0)

    for j in range(4):
        p.dma(ktmp[:], qkvgT[2048 + j * 128 : 2048 + j * 128 + 128, :])
        rope_to(rkT[:, j * S : (j + 1) * S], ktmp[:], 0, S)
        ov = kpcT[:].rearrange("p (i j2 c) -> p j2 i c", i=16, j2=4)[:, j]
        iv = ktmp[:].rearrange("p (c i) -> p i c", c=128)
        kp = kposT[:, j * 16 : j * 16 + 16].unsqueeze(-1).broadcast_to([128, 16, 128])
        p.tt(ov, iv, kp, ALU.add)
    compress(kw1, kw2, kb1, kb2, ckT)
    for j in range(4):
        p.dma(ktmp[:], qkvgT[2560 + j * 128 : 2560 + j * 128 + 128, :])
        ov = kpcT[:].rearrange("p (i j2 c) -> p j2 i c", i=16, j2=4)[:, j]
        iv = ktmp[:].rearrange("p (c i) -> p i c", c=128)
        vp = vposT[:, j * 16 : j * 16 + 16].unsqueeze(-1).broadcast_to([128, 16, 128])
        p.tt(ov, iv, vp, ALU.add)
        for tt_ in range(16):
            p.trans(ptr16[:], ktmp[:, tt_ * 128 : tt_ * 128 + 128], ident16[:])
            p.copy(V_all[:, j * S + tt_ * 128 : j * S + tt_ * 128 + 128], ptr16[:])
    compress(vw1, vw2, vb1, vb2, cvT)
    for j in range(4):
        p.copy(ckT[:, j * 129 : j * 129 + 1], memk[:, j : j + 1])
        p.trans(ptr16[:], cvT[:, j * 129 + 1 : j * 129 + 129], ident16[:])
        p.copy(cv128[:, j * 128 : j * 128 + 128], ptr16[:])

    graw = p_sb("graw", [48, S], BF16, sa_stack)
    gsig = p_sb("gsig", [48, S], F32, sa_stack)
    p.dma(graw[:], qkvgT[3072:3120, :])
    p.act(gsig[:], graw[:], AF.Sigmoid, bias=bgat[:], scale=1.0)
    for tt_ in range(16):
        p.trans(ptrf[:, 0:48], gsig[:, tt_ * 128 : tt_ * 128 + 128], identf[0:48, 0:48])
        p.copy(gatesT[:, tt_ * 48 : tt_ * 48 + 48], ptrf[:, 0:48])

    sa_stack.close()
    qTk = p_sb("qTk", [128, 4 * S], BF16)
    Praw = p_sb("Praw", [128, S], BF16)
    Pf = p_sb("Pf", [128, S], BF16)
    Pc = p_sb("Pc", [128, S], BF16)
    PTb = p_sb("PTb", [128, S], BF16)
    dscr = p_sb("dscr", [128, 128], BF16)
    dscr2 = p_sb("dscr2", [128, 128], BF16)
    gcomb = p_sb("gcomb", [128, 128], BF16)
    rqT = p_sb("rqT", [128, 128], BF16)
    impb = p_sb("impb", [128, 128], F32)
    impc = p_sb("impc", [128, 128], F32)
    pg = p_sb("pg", [128, 128], F32)
    gsel = p_sb("gsel", [128, 128], BF16)
    cpe = p_sb("cpe", [128, 129], F32)
    cpg = p_sb("cpg", [128, 129], F32)
    cps = p_sb("cps", [128, 129], BF16)
    ctb = p_sb("ctb", [128, 128], BF16)
    ctm = p_sb("ctm", [1, 128], BF16)
    ssc1 = p_sb("ssc1", [128, 128], BF16)
    ssc2 = p_sb("ssc2", [128, 128], BF16)
    cols = p_sb("colbuf", [128, 32], F32)
    m8 = p_sb("m8", [128, 8], F32)

    C_M, C_B, C_Z, C_RZ, C_T4, C_S1, C_S2, C_MS, C_BS, C_ZF, C_RZF, C_AF, \
        C_Z1, C_Z2, C_ZS, C_RZS, C_BF, C_MC, C_BC, C_ZC, C_RZC, C_CF, C_MP = range(23)

    def col(i):
        return cols[:, i : i + 1]

    for j in range(L0A_NKVH):
        p.dma(qTk[:].rearrange("p (h t) -> p h t", h=4),
              qkvgT[j * 512 : j * 512 + 512, :].rearrange("(h p) t -> p h t", h=4))
        for qt in range(L0A_NQT):
            nk = (qt + 1) * 128
            nch = (nk + 511) // 512
            for g in range(4):
                p.mm(psS[g][:, 0:129], qTk[:, g * S + qt * 128 : g * S + qt * 128 + 128],
                     ckT[:, j * 129 : j * 129 + 129])
                p.scopy(csimS[:, g * 129 : g * 129 + 129], psS[g][:, 0:129])
            p.tt(impb[:], csimS[:, 1:129], csimS[:, 129 + 1 : 129 + 129], ALU.add)
            p.tt(impc[:], csimS[:, 258 + 1 : 258 + 129], csimS[:, 387 + 1 : 387 + 129], ALU.add)
            p.tt(impb[:], impb[:], impc[:], ALU.add)
            p.rmax(col(C_M), impb[:])
            p.ts(col(C_B), col(C_M), -SCALE_ATT * 0.25, ALU.mult)
            p.act(pg[:], impb[:], AF.Exp, bias=col(C_B), scale=SCALE_ATT * 0.25)
            p.tt(pg[:], pg[:], cgate[:, qt * 129 + 1 : qt * 129 + 129], ALU.mult)
            p.rsum(col(C_Z), pg[:])
            p.ts(col(C_Z), col(C_Z), 1e-30, ALU.max)
            p.recip(col(C_RZ), col(C_Z))
            p.ts(pg[:], pg[:], col(C_RZ), ALU.mult)
            p.max8(m8[:], pg[:])
            p.ts(impb[:], pg[:], m8[:, 3:4], ALU.is_ge)
            p.ts(impc[:], pg[:], 1e-10, ALU.is_gt)
            p.tt(gsel[:], impb[:], impc[:], ALU.mult)
            for g in range(4):
                hloc = j * 4 + g
                gcol0 = qt * 48 + 3 * hloc
                rope_to(rqT[:], qTk[:, g * S + qt * 128 : g * S + qt * 128 + 128],
                        qt * 128, 128)
                for kc in range(nch):
                    N = min(512, nk - kc * 512)
                    p.mm(psS[kc][:, 0:N], rqT[:], rkT[:, j * S + kc * 512 : j * S + kc * 512 + N])
                    p.rmax(cols[:, C_MP + kc : C_MP + kc + 1], psS[kc][:, 0:N])
                p.rmax(col(C_MS), cols[:, C_MP : C_MP + nch])
                p.ts(col(C_BS), col(C_MS), -SCALE_ATT, ALU.mult)
                for kc in range(nch):
                    N = min(512, nk - kc * 512)
                    p.act(Praw[:, kc * 512 : kc * 512 + N], psS[kc][:, 0:N], AF.Exp,
                          bias=col(C_BS), scale=SCALE_ATT)
                nb = nk // 16
                p.tt(Pf[:, 0:nk].rearrange("p (b i) -> p b i", i=16),
                     Praw[:, 0:nk].rearrange("p (b i) -> p b i", i=16),
                     gsel[:, 0:nb].unsqueeze(-1).broadcast_to([128, nb, 16]), ALU.mult)
                p.tt(gcomb[:].rearrange("p (b i) -> p b i", i=16),
                     gsel[:, nb - 8 : nb].unsqueeze(-1).broadcast_to([128, 8, 16]),
                     ownd[:].rearrange("p (b i) -> p b i", i=16), ALU.add)
                p.tt(Pf[:, nk - 128 : nk], Praw[:, nk - 128 : nk], gcomb[:], ALU.mult)
                for kc in range(nch):
                    N = min(512, nk - kc * 512)
                    p.rsum(cols[:, C_MP + kc : C_MP + kc + 1], Pf[:, kc * 512 : kc * 512 + N])
                p.rsum(col(C_ZF), cols[:, C_MP : C_MP + nch])
                p.ts(col(C_ZF), col(C_ZF), 1e-30, ALU.max)
                p.recip(col(C_RZF), col(C_ZF))
                p.tt(col(C_AF), gatesT[:, gcol0 + 1 : gcol0 + 2], col(C_RZF), ALU.mult)
                p.tt(ssc1[:], Praw[:, nk - 128 : nk], bandd[:], ALU.mult)
                p.rsum(col(C_Z1), ssc1[:])
                if qt > 0:
                    p.tt(ssc2[:], Praw[:, nk - 256 : nk - 128], bandp[:], ALU.mult)
                    p.rsum(col(C_Z2), ssc2[:])
                    p.tt(col(C_ZS), col(C_Z1), col(C_Z2), ALU.add)
                else:
                    p.copy(col(C_ZS), col(C_Z1))
                p.ts(col(C_ZS), col(C_ZS), 1e-30, ALU.max)
                p.recip(col(C_RZS), col(C_ZS))
                p.tt(col(C_BF), gatesT[:, gcol0 + 2 : gcol0 + 3], col(C_RZS), ALU.mult)
                p.ts(Pc[:, 0:nk], Pf[:, 0:nk], col(C_AF), ALU.mult)
                p.ts(ssc1[:], ssc1[:], col(C_BF), ALU.mult)
                p.tt(dscr[:], Pc[:, nk - 128 : nk], ssc1[:], ALU.add)
                if qt > 0:
                    p.ts(ssc2[:], ssc2[:], col(C_BF), ALU.mult)
                    p.tt(dscr2[:], Pc[:, nk - 256 : nk - 128], ssc2[:], ALU.add)
                p.rmax(col(C_MC), csimS[:, g * 129 : g * 129 + 129])
                p.ts(col(C_BC), col(C_MC), -SCALE_ATT, ALU.mult)
                p.act(cpe[:], csimS[:, g * 129 : g * 129 + 129], AF.Exp,
                      bias=col(C_BC), scale=SCALE_ATT)
                p.tt(cpg[:], cpe[:], cgate[:, qt * 129 : qt * 129 + 129], ALU.mult)
                p.rsum(col(C_ZC), cpg[:])
                p.ts(col(C_ZC), col(C_ZC), 1e-30, ALU.max)
                p.recip(col(C_RZC), col(C_ZC))
                p.tt(col(C_CF), gatesT[:, gcol0 : gcol0 + 1], col(C_RZC), ALU.mult)
                p.ts(cps[:], cpg[:], col(C_CF), ALU.mult)
                ntile = nk // 128
                for jt in range(ntile):
                    if jt == ntile - 1:
                        src = dscr[:]
                    elif qt > 0 and jt == ntile - 2:
                        src = dscr2[:]
                    else:
                        src = Pc[:, jt * 128 : jt * 128 + 128]
                    p.trans(ptr16[:], src, ident16[:])
                    p.copy(PTb[:, jt * 128 : jt * 128 + 128], ptr16[:])
                p.trans(ptr16[:], cps[:, 1:129], ident16[:])
                p.copy(ctb[:], ptr16[:])
                p.trans(pt1[:], cps[:, 0:1], ident16[:])
                p.copy(ctm[:], pt1[:])
                for jt in range(ntile):
                    p.mm(ps_ao[:], V_all[:, j * S + jt * 128 : j * S + jt * 128 + 128],
                         PTb[:, jt * 128 : jt * 128 + 128], start=(jt == 0), stop=False)
                p.mm(ps_ao[:], cv128[:, j * 128 : j * 128 + 128], ctb[:], start=False, stop=False)
                p.mm(ps_ao[:], mv_s[0:1, j * 128 : j * 128 + 128], ctm[:], start=False, stop=True)
                p.copy(aoTkv[:, hloc % 4 * S + qt * 128 : hloc % 4 * S + qt * 128 + 128], ps_ao[:])
        p.dma(aoT[j * 512 : j * 512 + 512, :].rearrange("(h p) t -> p h t", h=4),
              aoTkv[:].rearrange("p (h t) -> p h t", h=4))
    stk.close()
    return p


def build_l0post(p, ctx, io):
    aoF, woT, ffni, x1o = io["aoF"], io["woT"], io["ffni"], io["x1o"]
    stk = ExitStack()
    _sb = p.sb
    p_sb = lambda n, s, d: _sb(n, s, d, stk)

    aos = p_sb("aos", [128, 32 * S], BF16)
    for k in range(32):
        p.dma(aos[:, k * S : k * S + S], aoF[k * 128 : k * 128 + 128, :])
    wos = p_sb("wos", [128, 32 * 512], BF16)
    fres = p_sb("fres", [128, 512], F32)
    facc = p_sb("facc2", [128, 512], F32)
    pso = p.pool["A"]
    for oc in range(4):
        for k in range(32):
            p.dma_g(wos[:, k * 512 : k * 512 + 512], woT[oc, k * 128 : k * 128 + 128, :])
        for tt in range(16):
            for k in range(32):
                p.mm(pso[:], aos[:, k * S + tt * 128 : k * S + tt * 128 + 128],
                     wos[:, k * 512 : k * 512 + 512], start=(k == 0), stop=(k == 31))
            p.dma(fres[:], ffni[tt * 128 : tt * 128 + 128, oc * 512 : oc * 512 + 512])
            p.tt(facc[:], pso[:], fres[:], ALU.add)
            p.dma(x1o[tt * 128 : tt * 128 + 128, oc * 512 : oc * 512 + 512], facc[:])
    stk.close()
    return p




def build_l1pre(p, ctx, io):
    GT = 256
    load_x1 = io["load_x1"]
    ipw, cwi, cbi, dbi, ali, dvi, nwi, uti, mci, normedT = (
        io["inprojT"], io["convw"], io["convb"], io["dtbias"], io["alog"],
        io["dvals"], io["nw"], io["utones"], io["maskaddC"], io["normedT"])
    ip16 = p.dint("ip16", [81, 128, 4096], BF16)
    acs_scr = p.dint("acs_scr", [64, 128], F32)
    stk = ExitStack()
    _sb = p.sb
    p_sb = lambda n, s, d: _sb(n, s, d, stk)

    identf, ident16 = ctx["identf"], ctx["ident16"]
    utones = p_sb("m_ut", [128, 128], F32)
    maskC = p_sb("m_mask", [128, 128], F32)
    convw = p_sb("m_cw", [128, 192], F32)
    convb = p_sb("m_cb", [128, 48], F32)
    dtb = p_sb("m_dtb", [64, 1], F32)
    alog = p_sb("m_alog", [64, 1], F32)
    acol = p_sb("m_acol", [64, 1], F32)
    dvals = p_sb("m_dv", [128, 64], F32)
    nw = p_sb("m_nw", [128, 32], F32)
    for dst, src in [(utones, uti), (maskC, mci), (convw, cwi),
                     (convb, cbi), (dtb, dbi), (alog, ali), (dvals, dvi), (nw, nwi)]:
        p.dma(dst[:], src)
    p.act(acol[:], alog[:], AF.Exp)
    p.ts(acol[:], acol[:], -1.0, ALU.mult)

    xtile = p_sb("m_xtile", [128, D], F32)
    xnb = p_sb("m_xnb", [128, D], BF16)
    stat = p_sb("m_stat", [128, 8], F32)
    xnT = p_sb("m_xnT", [128, 32 * GT], BF16)
    wcol = p_sb("m_wcol", [128, 32 * 128], BF16)
    projT = p_sb("m_projT", [128, 81 * GT], BF16)
    convT = p_sb("m_convT", [128, 48 * GT], BF16)
    ptail = p_sb("m_ptail", [128, 48 * 4], BF16)
    diagw = p_sb("m_diagw", [128, 128], BF16)
    dtF = p_sb("m_dtF", [128, GT], F32)
    dsc1 = p_sb("m_dsc1", [128, GT], F32)
    dsc2 = p_sb("m_dsc2", [128, GT], F32)
    gsil = p_sb("m_gsil", [128, 32 * 128], BF16)
    xg = p_sb("m_xg", [128, 32 * 128], BF16)
    sq = p_sb("m_sq", [128, 128], F32)
    nstg = p_sb("m_nstg", [128, 128], BF16)
    gn1 = p_sb("m_gn1", [1, 128], F32)

    Pst = p_sb("m_Pst", [128, 64 * 64], F32)
    Pbf = p_sb("m_Pbf", [128, 64 * 64], BF16)
    acs = p_sb("m_acs", [128, 64], F32)
    acsT = p_sb("m_acsT", [64, 128], F32)
    adtT = p_sb("m_adtT", [128, 64], F32)
    dtok = p_sb("m_dtok", [128, 64], F32)
    alrow = p_sb("m_al", [64, 4], F32)
    cdT = p_sb("m_cdT", [1, 64], F32)
    cdb = p_sb("m_cdb", [128, 64], F32)
    alb = p_sb("m_alb", [128, 64], F32)
    dscol = p_sb("m_dscol", [128, 2], F32)
    onesf = p_sb("m_ones", [1, 128], F32)
    onescol = p_sb("m_onesc", [128, 1], F32)
    acs_flat = p_sb("m_acsf", [1, 2048], F32)
    sdo4 = p_sb("m_sdo4", [128, 512], BF16)
    lamI4 = p_sb("m_lamI", [128, 512], F32)
    lamT4 = p_sb("m_lam4", [128, 512], BF16)
    STb = p_sb("m_STb", [128, 8 * 128], BF16)
    Btp = p_sb("m_Btp", [128, 8 * 128], BF16)
    MTb = p_sb("m_MT", [128, 128], BF16)
    Xh = p_sb("m_Xh", [128, 64], BF16)
    Bd = p_sb("m_Bd", [128, 128], BF16)
    Woff = p_sb("m_Woff", [128, 128], BF16)
    Dhs = p_sb("m_Dhs", [128, 128], BF16)

    psA, psB, psC = p.pool["A"], p.pool["B"], p.pool["C"]
    psT16, psTf, psY, psGN = p.pool["T16"], p.pool["TF"], p.pool["AO"], p.pool["P1"]

    p.memset(onesf[:], 1.0)
    p.memset(onescol[:], 1.0)
    p.memset(ptail[:], 0.0)
    p.memset(Pst[:], 0.0)
    p.memset(Pbf[:], 0.0)

    for tg in range(L1P_NTG):
        t0 = tg * GT
        for tt in range(2):
            load_x1(p, xtile, t0 + tt * 128)
            p.act(xnb[:], xtile[:], AF.Square, accum_out=stat[:, 1:2])
            p.rsum(stat[:, 0:1], xtile[:])
            p.ts(stat[:, 0:1], stat[:, 0:1], 1.0 / D, ALU.mult)
            p.ts(stat[:, 2:3], stat[:, 1:2], 1.0 / D, ALU.mult)
            p.ts(stat[:, 4:5], stat[:, 0:1], stat[:, 0:1], ALU.mult)
            p.tt(stat[:, 3:4], stat[:, 2:3], stat[:, 4:5], ALU.subtract)
            p.ts(stat[:, 3:4], stat[:, 3:4], 1e-5, ALU.add)
            p.add("scalar", lambda e, o=stat[:, 5:6], i=stat[:, 3:4]: e.sqrt(o, i))
            p.recip(stat[:, 6:7], stat[:, 5:6])
            p.ts(xnb[:], xtile[:], stat[:, 0:1], ALU.subtract, stat[:, 6:7], ALU.mult)
            for k in range(32):
                p.trans(psT16[:], xnb[:, k * 128 : k * 128 + 128], ident16[:])
                p.copy(xnT[:, k * GT + tt * 128 : k * GT + tt * 128 + 128], psT16[:])
        for ct in range(81):
            if tg == 0:
                p.dma_g(wcol[:].rearrange("p (k c) -> p k c", k=32),
                        ipw[ct].rearrange("(k p) c -> p k c", k=32))
                p.dma(ip16[ct], wcol[:])
            else:
                p.dma(wcol[:], ip16[ct])
            for k in range(32):
                p.mm(psA[:, 0:GT], wcol[:, k * 128 : k * 128 + 128],
                     xnT[:, k * GT : k * GT + GT], start=(k == 0), stop=(k == 31))
            p.copy(projT[:, ct * GT : ct * GT + GT], psA[:, 0:GT])
        for ctl in range(0 if "noconv" in L1P_FEAT else 48):
            for kk in range(4):
                p.ts(diagw[:], identf[:], convw[:, ctl * 4 + kk : ctl * 4 + kk + 1], ALU.mult)
                sh = kk - 3
                if sh == 0:
                    p.mm(psB[:, 0:GT], diagw[:], projT[:, ctl * GT : ctl * GT + GT],
                         start=False, stop=True)
                else:
                    nh = -sh
                    p.mm(psB[:, 0:nh], diagw[:], ptail[:, ctl * 4 + 4 + sh : ctl * 4 + 4],
                         start=(kk == 0), stop=False)
                    p.mm(psB[:, nh:GT], diagw[:], projT[:, ctl * GT : ctl * GT + GT + sh],
                         start=False, stop=False)
            p.act(convT[:, ctl * GT : ctl * GT + GT], psB[:, 0:GT], AF.Silu,
                  bias=convb[:, ctl : ctl + 1], scale=1.0)
        for ctl in range(48):
            p.copy(ptail[:, ctl * 4 + 1 : ctl * 4 + 4],
                   projT[:, ctl * GT + GT - 3 : ctl * GT + GT])
        p.ts(dsc1[0:64, :], projT[0:64, 80 * GT : 80 * GT + GT], dtb[:], ALU.add)
        p.act(dsc2[0:64, :], dsc1[0:64, :], AF.Square)
        p.add("scalar", lambda e, o=dsc2[0:64, :], i=dsc2[0:64, :]: e.sqrt(o, i))
        p.act(dsc2[0:64, :], dsc2[0:64, :], AF.Exp, bias=0.0, scale=-1.0)
        p.act(dsc2[0:64, :], dsc2[0:64, :], AF.Ln, bias=1.0, scale=1.0)
        p.ts(dsc1[0:64, :], dsc1[0:64, :], 0.0, ALU.max)
        p.tt(dtF[0:64, :], dsc1[0:64, :], dsc2[0:64, :], ALU.add)
        for cc in range([] if "noscan" in L1P_FEAT else range(2)) if False else range(0 if "noscan" in L1P_FEAT else 2):
            ch0 = cc * 128
            for k in range(32):
                p.act(gsil[:, k * 128 : k * 128 + 128],
                      projT[:, (48 + k) * GT + ch0 : (48 + k) * GT + ch0 + 128], AF.Silu)
            p.ts(dsc1[0:64, 0:128], dtF[0:64, ch0 : ch0 + 128], acol[:], ALU.mult)
            p.rsum(alrow[:, 0:1], dsc1[0:64, 0:128])
            p.act(alrow[:, 1:2], alrow[:, 0:1], AF.Exp)
            p.trans(psTf[:, 0:64], dsc1[0:64, 0:128], identf[0:64, 0:64])
            p.copy(adtT[:], psTf[:, 0:64])
            p.trans(psTf[:, 0:64], dtF[0:64, ch0 : ch0 + 128], identf[0:64, 0:64])
            p.copy(dtok[:], psTf[:, 0:64])
            p.mm(psTf[:, 0:64], utones[:], adtT[:])
            p.copy(acs[:, 0:64], psTf[:, 0:64])
            p.trans(psTf[0:64, 0:128], acs[:, 0:64], identf[:])
            p.copy(acsT[:], psTf[0:64, 0:128])
            p.trans(psGN[0:1, 0:64], alrow[:, 1:2], identf[0:64, 0:64])
            p.copy(cdT[:], psGN[0:1, 0:64])
            p.mm(psTf[:, 0:64], onesf[:], cdT[:])
            p.copy(cdb[:], psTf[:, 0:64])
            p.trans(psGN[0:1, 0:64], alrow[:, 0:1], identf[0:64, 0:64])
            p.copy(cdT[:], psGN[0:1, 0:64])
            p.mm(psTf[:, 0:64], onesf[:], cdT[:])
            p.copy(alb[:], psTf[:, 0:64])
            p.dma(acs_scr, acsT[:])
            for gg in range(8):
                bt = convT[:, (32 + gg) * GT + ch0 : (32 + gg) * GT + ch0 + 128]
                ctg = convT[:, (40 + gg) * GT + ch0 : (40 + gg) * GT + ch0 + 128]
                p.mm(psA[:, 0:128], bt, ctg)
                p.copy(STb[:, gg * 128 : gg * 128 + 128], psA[:, 0:128])
                p.trans(psT16[:], bt, ident16[:])
                p.copy(Btp[:, gg * 128 : gg * 128 + 128], psT16[:])
            for rcg in range(0 if "noheads" in L1P_FEAT else 4):
                p.dma(acs_flat[:],
                      acs_scr[rcg * 16 : rcg * 16 + 16, :].rearrange("h t -> (h t)").unsqueeze(0))
                for rc4 in range(4):
                    h0 = rcg * 16 + rc4 * 4
                    p.mm(psC[:], onesf[:], acs_flat[:, rc4 * 512 : rc4 * 512 + 512])
                    p.act(sdo4[:], psC[:], AF.Exp)
                    for hh in range(4):
                        h = h0 + hh
                        p.stt(lamI4[:, hh * 128 : hh * 128 + 128],
                              psC[:, hh * 128 : hh * 128 + 128], acs[:, h : h + 1],
                              maskC[:], ALU.subtract, ALU.add)
                    p.act(lamT4[:], lamI4[:], AF.Exp)
                    for hh in range(4):
                        h = h0 + hh
                        gg = h % 8
                        ctile = h // 2
                        rb = (h % 2) * 64
                        p.tt(MTb[:], STb[:, gg * 128 : gg * 128 + 128],
                             lamT4[:, hh * 128 : hh * 128 + 128], ALU.mult)
                        if rb == 0:
                            p.trans(psT16[:, 0:64],
                                    convT[0:64, ctile * GT + ch0 : ctile * GT + ch0 + 128],
                                    ident16[0:64, 0:64])
                        else:
                            p.trans(psT16[:, 0:64],
                                    convT[64:128, ctile * GT + ch0 : ctile * GT + ch0 + 128],
                                    ident16[64:128, 64:128])
                        p.ts(Xh[:], psT16[:, 0:64], dtok[:, h : h + 1], ALU.mult)
                        p.tt(dscol[:, 0:1], alb[:, h : h + 1], acs[:, h : h + 1], ALU.subtract)
                        p.act(dscol[:, 1:2], dscol[:, 0:1], AF.Exp)
                        p.ts(Bd[:], Btp[:, gg * 128 : gg * 128 + 128],
                             dscol[:, 1:2], ALU.mult)
                        p.tt(Woff[:], convT[:, (40 + gg) * GT + ch0 : (40 + gg) * GT + ch0 + 128],
                             sdo4[:, hh * 128 : hh * 128 + 128], ALU.mult)
                        py = psY[rb : rb + 64, 0:128]
                        p.mm(py, Xh[:], MTb[:], start=True, stop=False)
                        p.mm(py, Pbf[:, h * 64 : h * 64 + 64], Woff[:], start=False, stop=True)
                        p.mm(psB[:, 0:64], Bd[:], Xh[:])
                        p.ts(Pst[:, h * 64 : h * 64 + 64], Pst[:, h * 64 : h * 64 + 64],
                             cdb[:, h : h + 1], ALU.mult)
                        p.tt(Pst[:, h * 64 : h * 64 + 64], Pst[:, h * 64 : h * 64 + 64],
                             psB[:, 0:64], ALU.add)
                        p.copy(Pbf[:, h * 64 : h * 64 + 64], Pst[:, h * 64 : h * 64 + 64])
                        hsT = convT[rb : rb + 64, ctile * GT + ch0 : ctile * GT + ch0 + 128]
                        p.ts(Dhs[rb : rb + 64, :], hsT, dvals[rb : rb + 64, h : h + 1], ALU.mult)
                        p.tt(dsc1[rb : rb + 64, 0:128], py, Dhs[rb : rb + 64, :], ALU.add)
                        p.tt(xg[rb : rb + 64, ctile * 128 : ctile * 128 + 128],
                             dsc1[rb : rb + 64, 0:128],
                             gsil[rb : rb + 64, ctile * 128 : ctile * 128 + 128], ALU.mult)
            for ng in range(0 if "nogn" in L1P_FEAT else 4):
                for k in range(8):
                    ct8 = ng * 8 + k
                    p.act(sq[:], xg[:, ct8 * 128 : ct8 * 128 + 128], AF.Square)
                    p.mm(psGN[:], onescol[:], sq[:], start=(k == 0), stop=(k == 7))
                p.scopy(gn1[:], psGN[:])
                p.ts(gn1[:], gn1[:], 1.0 / 1024.0, ALU.mult)
                p.ts(gn1[:], gn1[:], 1e-5, ALU.add)
                p.add("scalar", lambda e, o=gn1[:], i=gn1[:]: e.sqrt(o, i))
                p.recip(gn1[:], gn1[:])
                p.mm(psC[:, 0:128], onesf[:], gn1[:])
                for k in range(8):
                    ct8 = ng * 8 + k
                    p.stt(nstg[:], xg[:, ct8 * 128 : ct8 * 128 + 128], nw[:, ct8 : ct8 + 1],
                          psC[:, 0:128], ALU.mult, ALU.mult)
                    p.dma(normedT[ct8 * 128 : ct8 * 128 + 128, t0 + ch0 : t0 + ch0 + 128],
                          nstg[:])
    stk.close()
    return p


def build_l1post(p, ctx, io):
    load_x1 = io["load_x1"]
    x1o, nrmf, opw, w1w, b1i, lAi, lBi, coefi, x2o = (
        io["p_x1o"], io["p_nrm"], io["opT"], io["w1T"], io["bias1"],
        io["loraAT"], io["loraBT"], io["coef"], io["x2o"])
    xn1s = p.dint("xn1s", [4096, S], BF16)
    identf, ident16 = ctx["identf"], ctx["ident16"]
    stk = ExitStack()
    _sb = p.sb
    p_sb = lambda n, s, d: _sb(n, s, d, stk)

    xstg = p_sb("q_xstg", [128, 32 * 128], BF16)
    xtile = p_sb("q_xtile", [128, D], F32)
    xnb = p_sb("q_xnb", [128, D], BF16)
    stat = p_sb("q_stat", [128, 8], F32)
    hT = p_sb("q_hT", [128, S], BF16)
    lA = p_sb("q_lA", [128, 32 * 128], BF16)
    lB = p_sb("q_lB", [128, 2048], BF16)
    coef = p_sb("q_coef", [128, 1], F32)
    psT16 = p.pool["T16"]
    psA = p.pool["A"]
    p.dma_g(lA[:].rearrange("p (k c) -> p k c", k=32),
            lAi.rearrange("(k p) c -> p k c", k=32))
    p.dma_g(lB[:], lBi)
    p.dma(coef[:], coefi)
    p.ts(lB[:], lB[:], coef[:], ALU.mult, 2.0, ALU.mult)

    for tt in range(16):
        load_x1(p, xtile, tt * 128)
        p.act(xnb[:], xtile[:], AF.Square, accum_out=stat[:, 1:2])
        p.rsum(stat[:, 0:1], xtile[:])
        p.ts(stat[:, 0:1], stat[:, 0:1], 1.0 / D, ALU.mult)
        p.ts(stat[:, 2:3], stat[:, 1:2], 1.0 / D, ALU.mult)
        p.ts(stat[:, 4:5], stat[:, 0:1], stat[:, 0:1], ALU.mult)
        p.tt(stat[:, 3:4], stat[:, 2:3], stat[:, 4:5], ALU.subtract)
        p.ts(stat[:, 3:4], stat[:, 3:4], 1e-5, ALU.add)
        p.add("scalar", lambda e, o=stat[:, 5:6], i=stat[:, 3:4]: e.sqrt(o, i))
        p.recip(stat[:, 6:7], stat[:, 5:6])
        p.ts(xnb[:], xtile[:], stat[:, 0:1], ALU.subtract, stat[:, 6:7], ALU.mult)
        for k in range(32):
            p.trans(psT16[:], xnb[:, k * 128 : k * 128 + 128], ident16[:])
            p.copy(xstg[:, k * 128 : k * 128 + 128], psT16[:])
        for k in range(32):
            p.dma(xn1s[k * 128 : k * 128 + 128, tt * 128 : tt * 128 + 128],
                  xstg[:, k * 128 : k * 128 + 128])
        for k in range(32):
            p.mm(psA[:, 0:128], lA[:, k * 128 : k * 128 + 128],
                 xstg[:, k * 128 : k * 128 + 128], start=(k == 0), stop=(k == 31))
        p.copy(hT[:, tt * 128 : tt * 128 + 128], psA[:, 0:128])

    nrow = p_sb("q_nrow", [128, 512], BF16)
    wo1 = p_sb("q_wo1", [128, 64 * 512], BF16)
    ww1 = p_sb("q_ww1", [128, 32 * 512], BF16)
    b1row = p_sb("q_b1", [1, 2048], BF16)
    ones1 = p_sb("q_ones1", [1, 128], BF16)
    xres = p_sb("q_xres", [128, 512], F32)
    facc = p_sb("q_facc", [128, 512], F32)
    pso = [p.pool["A"], p.pool["B"], p.pool["C"], p.pool["D"]]
    p.memset(ones1[:], 1.0)
    p.dma_g(b1row[:], b1i)
    for oc in range(4):
        for k in range(64):
            p.dma_g(wo1[:, k * 512 : k * 512 + 512], opw[oc, k * 128 : k * 128 + 128, :])
        for k in range(32):
            p.dma_g(ww1[:, k * 512 : k * 512 + 512], w1w[oc, k * 128 : k * 128 + 128, :])
        for ttb in range(4):
            tb0 = ttb * 512
            for k in range(64):
                p.dma(nrow[:], nrmf[k * 128 : k * 128 + 128, tb0 : tb0 + 512])
                for ti in range(4):
                    p.mm(pso[ti][:], nrow[:, ti * 128 : ti * 128 + 128],
                         wo1[:, k * 512 : k * 512 + 512], start=(k == 0), stop=False)
            for k in range(32):
                p.dma(nrow[:], xn1s[k * 128 : k * 128 + 128, tb0 : tb0 + 512])
                for ti in range(4):
                    p.mm(pso[ti][:], nrow[:, ti * 128 : ti * 128 + 128],
                         ww1[:, k * 512 : k * 512 + 512], start=False, stop=False)
            for ti in range(4):
                tt = ttb * 4 + ti
                p.mm(pso[ti][:], hT[:, tt * 128 : tt * 128 + 128],
                     lB[:, oc * 512 : oc * 512 + 512], start=False, stop=False)
                p.mm(pso[ti][:], ones1[:], b1row[:, oc * 512 : oc * 512 + 512],
                     start=False, stop=True)
                p.dma(xres[:], x1o[tt * 128 : tt * 128 + 128, oc * 512 : oc * 512 + 512])
                p.tt(facc[:], pso[ti][:], xres[:], ALU.add)
                p.dma(x2o[tt * 128 : tt * 128 + 128, oc * 512 : oc * 512 + 512], facc[:])
    stk.close()
    return p


_PROGS = {}
DEBUG_CAPTURE = None
PAIRS = [[0, 1], [2, 3], [4, 5], [6, 7]]


def _io_l0pre(p, internal_qkvg):
    io = dict(
        xb=p.din("xb", [S, D], F32),
        xbf=p.din("xbf", [S, 2048], F32),
        wqkvgT=p.din("wqkvgT", [25, D, 128], F32),
        qweight=p.din("qweight", [2048, 4096], I32),
        qscales=p.din("qscales", [2048, 32], F32),
        qzeros=p.din("qzeros", [2048, 32], F32),
        fbias=p.din("fbias", [1, 2048], F32),
        ffnacc=p.dout("ffnacc", [S, 2048], F32),
    )
    io["qkvgT"] = p.dint("qkvgT_i", [3200, S], BF16) if internal_qkvg else \
        p.dout("qkvgT", [3200, S], BF16)
    return io


def _io_l0attn(p, qkvgT):
    return dict(
        qkvgT=qkvgT,
        kw1T=p.din("kw1T", [16, 2048, 128], F32),
        vw1T=p.din("vw1T", [16, 2048, 128], F32),
        kw2T=p.din("kw2T", [2048, 128], F32),
        vw2T=p.din("vw2T", [2048, 128], F32),
        kb1=p.din("kb1", [128, 16], F32),
        vb1=p.din("vb1", [128, 16], F32),
        kb2=p.din("kb2", [128, 1], F32),
        vb2=p.din("vb2", [128, 1], F32),
        kposT=p.din("kposT", [128, 64], F32),
        vposT=p.din("vposT", [128, 64], F32),
        memk=p.din("memk", [128, 4], F32),
        memv=p.din("memv", [1, 512], F32),
        b_gate=p.din("b_gate", [48, 1], F32),
        ropC=p.din("ropC", [128, S], F32),
        ropS=p.din("ropS", [128, S], F32),
        cgate=p.din("cgate", [128, 16 * 129], BF16),
        owndiag=p.din("owndiag", [128, 128], BF16),
        banddiag=p.din("banddiag", [128, 128], BF16),
        bandprev=p.din("bandprev", [128, 128], BF16),
        swp=p.din("swp", [128, 128], F32),
        aoT=p.dout("aoT", [2048, S], BF16),
    )


def _build_launchA(p):
    ctx = _ctx(p)
    ioP = _io_l0pre(p, internal_qkvg=True)
    ioA = _io_l0attn(p, ioP["qkvgT"])
    build_l0pre(p, ctx, ioP)
    build_l0attn(p, ctx, ioA)
    return p


def _build_launchB(p):
    ctx = _ctx(p)
    aoF = p.din("aoF", [4096, S], BF16)
    x1i = p.dint("x1i", [S, 2048], F32)
    ioPost = dict(aoF=aoF, woT=p.din("woT", [4, 4096, 512], F32),
                  ffni=p.din("ffni", [S, 2048], F32), x1o=x1i)
    build_l0post(p, ctx, ioPost)
    x1ag = p.dint("x1ag", [2 * S, 2048], F32)
    p.collective("AllGather", ALU.bypass, PAIRS, [x1i.opt()], [x1ag.opt()])

    def load_x1(pp, dst, row):
        pp.dma(dst[:, 0:2048], x1ag[row : row + 128, :])
        pp.dma(dst[:, 2048:4096], x1ag[S + row : S + row + 128, :])

    nrm_i = p.dint("nrm_i", [4096, S], BF16)
    ioM = dict(load_x1=load_x1,
               inprojT=p.din("inprojT", [81, D, 128], F32),
               convw=p.din("convw", [128, 192], F32),
               convb=p.din("convb", [128, 48], F32),
               dtbias=p.din("dtbias", [64, 1], F32),
               alog=p.din("alog", [64, 1], F32),
               dvals=p.din("dvals", [128, 64], F32),
               nw=p.din("nw", [128, 32], F32),
               utones=p.din("utones", [128, 128], F32),
               maskaddC=p.din("maskaddC", [128, 128], F32),
               normedT=nrm_i)
    build_l1pre(p, ctx, ioM)
    nrmag = p.dint("nrmag", [8192, S], BF16)
    p.collective("AllGather", ALU.bypass, PAIRS, [nrm_i.opt()], [nrmag.opt()])
    ioQ = dict(load_x1=load_x1, p_x1o=x1i, p_nrm=nrmag,
               opT=p.din("opT", [4, 8192, 512], F32),
               w1T=p.din("w1T", [4, 4096, 512], F32),
               bias1=p.din("bias1", [1, 2048], F32),
               loraAT=p.din("loraAT", [4096, 128], F32),
               loraBT=p.din("loraBT", [128, 2048], F32),
               coef=p.din("coef", [128, 1], F32),
               x2o=p.dout("x2o", [S, 2048], F32))
    build_l1post(p, ctx, ioQ)
    return p


def _prog(name, builder):
    if name not in _PROGS:
        p = P(name)
        builder(p)
        _PROGS[name] = (p.finalize(), len(p.steps))
    return _PROGS[name][0]


def _imA(gs, C):
    return [dict(xb=g["xb"], xbf=g["xbf"], wqkvgT=g["wqkvgT"], qweight=g["qweight"],
                 qscales=g["qscales"], qzeros=g["qzeros"], fbias=g["fbias"],
                 identf=C["identf"], kw1T=g["kw1T"], vw1T=g["vw1T"], kw2T=g["kw2T"],
                 vw2T=g["vw2T"], kb1=g["kb1"], vb1=g["vb1"], kb2=g["kb2"], vb2=g["vb2"],
                 kposT=g["kposT"], vposT=g["vposT"], memk=g["memk"], memv=g["memv"],
                 b_gate=g["b_gate"], ropC=C["ropC"], ropS=C["ropS"], cgate=C["cgate"],
                 owndiag=C["owndiag"], banddiag=C["banddiag"], bandprev=C["bandprev"],
                 swp=C["swp"]) for g in gs]


def _imB(gs, C, aoT, ffnacc):
    im = []
    for c, g in enumerate(gs):
        pr = (c // 2) * 2
        aoF = np.ascontiguousarray(np.concatenate([aoT[pr], aoT[pr + 1]], 0))
        im.append(dict(aoF=aoF, woT=g["woT"], ffni=ffnacc[c], identf=C["identf"],
                       inprojT=g["inprojT"], convw=g["convw"], convb=g["convb"],
                       dtbias=g["dtbias"], alog=g["alog"], dvals=g["dvals"], nw=g["nw"],
                       utones=C["utones"], maskaddC=C["maskaddC"], opT=g["opT"],
                       w1T=g["w1T"], bias1=g["bias1"], loraAT=g["loraAT"],
                       loraBT=g["loraBT"], coef=g["coef"]))
    return im


def kernel(**inputs):
    C = _consts()
    gs = [prep_core(inputs, c) for c in range(NCORE)]
    cap = DEBUG_CAPTURE if DEBUG_CAPTURE is not None else {}

    ncA = _prog("launchA", _build_launchA)
    rA = run_bass_kernel_spmd(ncA, _imA(gs, C), core_ids=list(range(NCORE))).results
    aoT = [np.asarray(r["aoT"]) for r in rA]
    ffnacc = [np.asarray(r["ffnacc"]) for r in rA]
    cap["aoT"] = aoT
    cap["ffnacc"] = ffnacc

    ncB = _prog("launchB", _build_launchB)
    rB = run_bass_kernel_spmd(ncB, _imB(gs, C, aoT, ffnacc), core_ids=list(range(NCORE))).results
    x2o = [np.asarray(r["x2o"]) for r in rB]
    cap["x2o"] = x2o

    out = np.empty((B, S, D), np.float32)
    for b in range(B):
        out[b, :, 0:2048] = x2o[2 * b]
        out[b, :, 2048:4096] = x2o[2 * b + 1]
    return out
